# version 09
# baseline (speedup 1.0000x reference)
import numpy as np
from contextlib import ExitStack

import concourse.bass as bass
import concourse.mybir as mybir
from concourse.bass_utils import run_bass_kernel_spmd

F32 = mybir.dt.float32
BF16 = mybir.dt.bfloat16
I32 = mybir.dt.int32
ALU = mybir.AluOpType
AF = mybir.ActivationFunctionType
AX = mybir.AxisListType.X

M23 = float(np.float32(1.5 * 2 ** 23))
M52 = float(np.float32(1.5 * 2 ** 52))

D = 1024
H = 16
DH = 64
NA = 64
HID = 4096
KC = D // 128
MC1 = HID // 128


class Sem:
    def __init__(self, nc, name):
        self.h = nc.alloc_semaphore(name)
        self.n = 0


def build_nc(n_tokens=4096, debug_taps=False, max_stage=3, pre_stage=99, wsc_iters=64, no_agg=False, no_vext=False, no_sm=False, max_blks=9999, out_stage=None):
    nc = bass.Bass("TRN2", target_bir_lowering=False, detect_race_conditions=False)
    N = n_tokens
    NBLK = N // 128
    NTA = N // 512
    NE = N // 512

    x_fm = nc.dram_tensor("x_fm", [D, N], F32, kind="ExternalInput")
    wemb_qk = nc.dram_tensor("wemb_qk", [D, D], F32, kind="ExternalInput")
    wembT_v = nc.dram_tensor("wembT_v", [D, D], F32, kind="ExternalInput")
    wanchT = nc.dram_tensor("wanchT", [DH, NA], F32, kind="ExternalInput")
    woutT = nc.dram_tensor("woutT", [D, D], F32, kind="ExternalInput")
    bout_pc = nc.dram_tensor("bout_pc", [128, KC], F32, kind="ExternalInput")
    w1T = nc.dram_tensor("w1T", [D, HID], F32, kind="ExternalInput")
    b1_pc = nc.dram_tensor("b1_pc", [128, MC1], F32, kind="ExternalInput")
    w2T = nc.dram_tensor("w2T", [HID, D], F32, kind="ExternalInput")
    b2_pc = nc.dram_tensor("b2_pc", [128, KC], F32, kind="ExternalInput")
    iden = nc.dram_tensor("iden", [128, 128], BF16, kind="ExternalInput")
    z_fm = nc.dram_tensor("z_fm", [D, N], F32, kind="ExternalOutput")
    w1q_dram = nc.dram_tensor("w1q_dram", [MC1, 128, D], BF16, kind="Internal")
    w2q_dram = nc.dram_tensor("w2q_dram", [KC, 128, HID], BF16, kind="Internal")
    y1_dram = nc.dram_tensor("y1_dram", [D, N], BF16, kind="Internal")
    dbg = {}
    if debug_taps:
        dbg["attn"] = nc.dram_tensor("dbg_attn", [128, NBLK * 1024], BF16, kind="ExternalOutput")
        dbg["avs"] = nc.dram_tensor("dbg_avs", [128, 1024], BF16, kind="ExternalOutput")
        dbg["y1"] = nc.dram_tensor("dbg_y1", [128, KC * N], BF16, kind="ExternalOutput")
        dbg["mixed"] = nc.dram_tensor("dbg_mixed", [128, 8 * 512], BF16, kind="ExternalOutput")
        dbg["t"] = nc.dram_tensor("dbg_t", [128, KC * 512], BF16, kind="ExternalOutput")
        dbg["wsc"] = nc.dram_tensor("dbg_wsc", [128, KC * 1024], BF16, kind="ExternalOutput")
        dbg["scores"] = nc.dram_tensor("dbg_scores", [128, 1024], F32, kind="ExternalOutput")
        dbg["vext"] = nc.dram_tensor("dbg_vext", [128, 8 * 132], BF16, kind="ExternalOutput")
        dbg["aggps"] = nc.dram_tensor("dbg_aggps", [128, 8 * 132], F32, kind="ExternalOutput")
        dbg["recs"] = nc.dram_tensor("dbg_recs", [128, 8], F32, kind="ExternalOutput")
        dbg["vxall"] = nc.dram_tensor("dbg_vxall", [128, 4 * (n_tokens // 128)], BF16, kind="ExternalOutput")
        dbg["psv"] = nc.dram_tensor("dbg_psv", [128, 1024], F32, kind="ExternalOutput")
        dbg["r2tap"] = nc.dram_tensor("dbg_r2tap", [128, KC * 512], BF16, kind="ExternalOutput")
        dbg["zstap"] = nc.dram_tensor("dbg_zstap", [128, 512], F32, kind="ExternalOutput")
        dbg["w1m"] = nc.dram_tensor("dbg_w1m", [128, 1024], BF16, kind="ExternalOutput")
        dbg["w2m"] = nc.dram_tensor("dbg_w2m", [128, 4096], BF16, kind="ExternalOutput")
        dbg["ln2"] = nc.dram_tensor("dbg_ln2", [128, 1024], F32, kind="ExternalOutput")
        dbg["ln2rows"] = nc.dram_tensor("dbg_ln2rows", [1, 2048], F32, kind="ExternalOutput")
        dbg["h"] = nc.dram_tensor("dbg_h", [128, MC1 * 512], BF16, kind="ExternalOutput")

    es = ExitStack()

    def sbuf(stack, name, shape, dt, side="right"):
        return stack.enter_context(nc.sbuf_tensor(name, shape, dt, side=side))

    def psum_(stack, name, shape, dt=F32):
        return stack.enter_context(nc.psum_tensor(name, shape, dt))

    idt = sbuf(es, "idt", [128, 128], BF16, side="left")
    onesb = sbuf(es, "onesb", [128, 128], BF16, side="left")
    fflag = sbuf(es, "fflag", [128, 1], F32, side="left")
    fscr = sbuf(es, "fscr", [128, 1], F32, side="left")
    bout_sb = sbuf(es, "bout_sb", [128, KC], F32, side="left")
    b1e = sbuf(es, "b1e", [128, MC1], F32, side="left")
    b2e = sbuf(es, "b2e", [128, KC], F32, side="left")
    avs = sbuf(es, "avs", [128, 8 * 128], BF16, side="left")
    recs = sbuf(es, "recs", [128, 8], F32, side="left")
    recs0 = sbuf(es, "recs0", [128, 8], F32, side="left")

    meanr = sbuf(es, "meanr", [1, 512], F32, side="left")
    varr = sbuf(es, "varr", [1, 512], F32, side="left")
    msqr = sbuf(es, "msqr", [1, 512], F32, side="left")
    rsigr = sbuf(es, "rsigr", [1, 512], F32, side="left")
    abrow = sbuf(es, "abrow", [1, 1024], F32, side="left")
    onesf = sbuf(es, "onesf", [1, 128], F32, side="left")
    wstg = [sbuf(es, "wstg0", [128, 1024], F32, side="left"), sbuf(es, "wstg1", [128, 1024], F32, side="left")]

    attn_scope = ExitStack()

    ops = {k: [] for k in ("sync", "scalar", "vector", "tensor", "gpsimd")}

    sSy = Sem(nc, "sSy")
    sAc = Sem(nc, "sAc")
    sPo = Sem(nc, "sPo")
    cPE = Sem(nc, "cPE")
    cV = Sem(nc, "cV")
    cS = Sem(nc, "cS")
    cG = Sem(nc, "cG")

    def op(eng, fn, inc=None):
        if inc is None:
            ops[eng].append(fn)
            return None
        sem, k = inc
        sem.n += k
        ops[eng].append(lambda e, fn=fn, s=sem.h, k=k: fn(e).then_inc(s, k))
        return sem.n

    def W(eng, sem, val):
        if val:
            ops[eng].append(lambda e, s=sem.h, v=val: e.wait_ge(s, v))

    def dma(eng, qsem, out_ap, in_ap):
        return op(eng, lambda e, o=out_ap, i=in_ap: e.dma_start(o, i), inc=(qsem, 16))

    scopes = []

    def _finish():
        op("sync", lambda e, v=sSy.n: e.wait_ge(sSy.h, v))
        op("gpsimd", lambda e, v=sSy.n: e.wait_ge(sSy.h, v))
        if sAc.n:
            op("gpsimd", lambda e, v=sAc.n: e.wait_ge(sAc.h, v))
        if sPo.n:
            op("gpsimd", lambda e, v=sPo.n: e.wait_ge(sPo.h, v))
        with nc.Block() as block:
            @block.sync
            def _(e):
                for fn in ops["sync"]:
                    fn(e)

            @block.scalar
            def _(e):
                for fn in ops["scalar"]:
                    fn(e)

            @block.vector
            def _(e):
                for fn in ops["vector"]:
                    fn(e)

            @block.tensor
            def _(e):
                for fn in ops["tensor"]:
                    fn(e)

            @block.gpsimd
            def _(e):
                for fn in ops["gpsimd"]:
                    fn(e)
        for s in scopes:
            s.close()
        es.close()
        return nc

    pa = ExitStack()
    wsc = sbuf(pa, "wsc", [128, KC * 1024], BF16, side="left")
    wembv = sbuf(pa, "wembv", [128, KC * 1024], BF16, side="left")
    pre = ExitStack()
    qstg = [sbuf(pre, "qstg0", [128, 128], F32), sbuf(pre, "qstg1", [128, 128], F32)]
    qtile = [sbuf(pre, "qtile0", [128, 128], BF16), sbuf(pre, "qtile1", [128, 128], BF16)]
    wanch_f = sbuf(pre, "wanch_f", [128, NA], F32)
    wanch2 = sbuf(pre, "wanch2", [128, NA], BF16)
    b1r = sbuf(pre, "b1r", [128, MC1], F32)
    b2r = sbuf(pre, "b2r", [128, KC], F32)
    bscr = sbuf(pre, "bscr", [128, MC1], F32)
    bscri = sbuf(pre, "bscri", [128, MC1], I32)
    wqA = sbuf(pre, "wqA", [128, 1024], F32)
    wqI = sbuf(pre, "wqI", [128, 1024], I32)
    wqbf = [sbuf(pre, "wqbf0", [128, 1024], BF16), sbuf(pre, "wqbf1", [128, 1024], BF16)]
    ps_wsc = [psum_(pre, "ps_wsc0", [128, 64]), psum_(pre, "ps_wsc1", [128, 64])]
    ps_wscB = [psum_(pre, "ps_wscB0", [128, 64]), psum_(pre, "ps_wscB1", [128, 64])]
    v_idt = dma("sync", sSy, idt[:], iden.ap())
    v_bout = dma("sync", sSy, bout_sb[:], bout_pc.ap())
    v_b1 = dma("sync", sSy, b1r[:], b1_pc.ap())
    v_b2 = dma("sync", sSy, b2r[:], b2_pc.ap())
    dma("sync", sSy, wanch_f[0:64, :], wanchT.ap())
    v_wa = dma("sync", sSy, wanch_f[64:128, :], wanchT.ap())

    op("vector", lambda e: e.memset(onesb[:], 1.0))
    op("vector", lambda e: e.memset(onesf[:], 1.0))
    v_ = op("vector", lambda e: e.memset(fscr[:], 1.0), inc=(cV, 1))
    W("vector", cV, v_)
    v_ = op("vector", lambda e: e.tensor_single_scalar(fscr[:], fscr[:], float(2.0 ** -30), ALU.add), inc=(cV, 1))
    W("vector", cV, v_)
    vff = op("vector", lambda e: e.tensor_scalar(fflag[:], fscr[:], -1.0, 0.0, ALU.add, ALU.is_equal), inc=(cV, 1))
    W("vector", cV, vff)
    W("vector", sSy, v_wa)
    v_wanch = op("vector", lambda e: e.tensor_single_scalar(wanch2[:], wanch_f[:], 0.125, ALU.mult), inc=(cV, 1))

    def bias_quant(src, dst, scale_out, width):
        v = op("vector", lambda e, w=width: e.tensor_scalar(bscr[:, 0:w], src[:, 0:w], 64.0, M23, ALU.mult, ALU.add), inc=(cV, 1))
        W("vector", cV, v)
        v = op("vector", lambda e, w=width: e.tensor_scalar(bscr[:, 0:w], bscr[:, 0:w], -M23, 127.0, ALU.add, ALU.min), inc=(cV, 1))
        W("vector", cV, v)
        v = op("vector", lambda e, w=width: e.tensor_single_scalar(bscri[:, 0:w], bscr[:, 0:w], -128.0, ALU.max), inc=(cV, 1))
        W("vector", cV, v)
        v = op("vector", lambda e, w=width: e.tensor_copy(bscr[:, 0:w], bscri[:, 0:w]), inc=(cV, 1))
        W("vector", cV, v)
        v = op("vector", lambda e, w=width: e.tensor_single_scalar(dst[:, 0:w], bscr[:, 0:w], scale_out, ALU.mult), inc=(cV, 1))
        W("vector", cV, v)

    W("vector", sSy, v_b1)
    bias_quant(b1r, b1e, 0.25, MC1)
    W("vector", sSy, v_b2)
    bias_quant(b2r, b2e, 1.0 / 64.0, KC)
    if pre_stage == 0:
        scopes.extend([pre, pa, attn_scope])
        return _finish()

    qk_cast = []
    wsc_evac = []
    wsc_mm = []
    for i in range(wsc_iters):
        hp, kc = i // KC, i % KC
        b = i % 2
        if i >= 2:
            W("sync", cV, qk_cast[i - 2])
        vd = dma("sync", sSy, qstg[b][:], wemb_qk.ap()[hp * 128:(hp + 1) * 128, kc * 128:(kc + 1) * 128])
        W("vector", sSy, vd)
        if i >= 2:
            W("vector", cPE, wsc_mm[i - 2])
        qk_cast.append(op("vector", lambda e, b=b: e.tensor_copy(qtile[b][:], qstg[b][:]), inc=(cV, 1)))
        W("tensor", cV, qk_cast[i])
        if i == 0:
            W("tensor", cV, v_wanch)
        if i >= 1:
            W("tensor", cV, wsc_evac[i - 1])
        op("tensor", lambda e, b=b: e.matmul(ps_wsc[b][:], qtile[b][0:64, :], wanch2[0:64, :],
                                             tile_position=(0, 0), start=True, stop=True))
        vmm = op("tensor", lambda e, b=b: e.matmul(ps_wscB[b][:], qtile[b][64:128, :], wanch2[64:128, :],
                                                   tile_position=(64, 0), start=True, stop=True), inc=(cPE, 1))
        wsc_mm.append(vmm)
        W("vector", cPE, vmm)
        op("vector", lambda e, b=b, hp=hp, kc=kc: e.tensor_copy(
            wsc[:, kc * 1024 + hp * 128: kc * 1024 + hp * 128 + 64], ps_wsc[b][:]))
        wsc_evac.append(op("vector", lambda e, b=b, hp=hp, kc=kc: e.tensor_copy(
            wsc[:, kc * 1024 + hp * 128 + 64: kc * 1024 + (hp + 1) * 128], ps_wscB[b][:]), inc=(cV, 1)))
    v_wsc_all = wsc_evac[-1]
    if pre_stage == 1:
        scopes.extend([pre, pa, attn_scope])
        return _finish()

    wembv_cast = []
    for kc in range(KC):
        b = kc % 2
        if kc >= 2:
            W("sync", cV, wembv_cast[kc - 2])
        vd = dma("sync", sSy, wstg[b][:, 0:1024], wembT_v.ap()[kc * 128:(kc + 1) * 128, :])
        W("vector", sSy, vd)
        wembv_cast.append(op("vector", lambda e, b=b, kc=kc: e.tensor_copy(
            wembv[:, kc * 1024:(kc + 1) * 1024], wstg[b][:, 0:1024]), inc=(cV, 1)))
    v_wembv_all = wembv_cast[-1]
    if pre_stage == 2:
        scopes.extend([pre, pa, attn_scope])
        return _finish()

    wq_store = []
    widx = 0

    def quant_chunk(b, ncols, dst_ap):
        nonlocal widx
        op("vector", lambda e, b=b, n=ncols: e.tensor_scalar(wqA[:, 0:n], wstg[b][:, 0:n], 64.0, M23, ALU.mult, ALU.add))
        op("vector", lambda e, b=b, n=ncols: e.tensor_scalar(wqA[:, 0:n], wqA[:, 0:n], -M23, 127.0, ALU.add, ALU.min))
        vqi = op("vector", lambda e, n=ncols: e.tensor_single_scalar(wqI[:, 0:n], wqA[:, 0:n], -128.0, ALU.max), inc=(cV, 1))
        W("vector", cV, vqi)
        vq = op("vector", lambda e, b=b, n=ncols: e.tensor_copy(wqbf[b][:, 0:n], wqI[:, 0:n]), inc=(cV, 1))
        W("scalar", cV, vq)
        wq_store.append(dma("scalar", sAc, dst_ap, wqbf[b][:, 0:ncols]))
        widx += 1

    for m in range(MC1):
        b = widx % 2
        if widx >= 2:
            W("sync", sAc, wq_store[widx - 2])
        in_ap = w1T.ap()[:, m * 128:(m + 1) * 128].rearrange("(kc p) c -> p kc c", p=128)
        vd = dma("sync", sSy, wstg[b][:, 0:1024].rearrange("p (kc c) -> p kc c", c=128), in_ap)
        W("vector", sSy, vd)
        quant_chunk(b, 1024, w1q_dram.ap()[m])
    for m2 in range(KC):
        for qq in range(4):
            b = widx % 2
            if widx >= 2:
                W("sync", sAc, wq_store[widx - 2])
            in_ap = w2T.ap()[qq * 1024:(qq + 1) * 1024, m2 * 128:(m2 + 1) * 128].rearrange(
                "(kk p) c -> p kk c", p=128)
            vd = dma("sync", sSy, wstg[b][:, 0:1024].rearrange("p (kk c) -> p kk c", c=128), in_ap)
            W("vector", sSy, vd)
            quant_chunk(b, 1024, w2q_dram.ap()[m2][:, qq * 1024:(qq + 1) * 1024])
    v_wq_all = wq_store[-1]
    if pre_stage == 3:
        scopes.extend([pre, pa, attn_scope])
        return _finish()

    pre.close()
    attn = sbuf(attn_scope, "attn", [128, (N // 128) * 1024], BF16)
    paa = ExitStack()
    agg_ps = [psum_(paa, "agg0", [128, 512]), psum_(paa, "agg1", [128, 512]),
              psum_(paa, "agg2", [128, 264])]
    ps_sc = psum_(paa, "ps_sc", [128, 1024])
    ps_v = psum_(paa, "ps_v", [128, 1024])
    xstg = [sbuf(paa, "xstg0", [128, 512], F32), sbuf(paa, "xstg1", [128, 512], F32)]
    x_t = [sbuf(paa, "x_t0", [128, KC * 512], BF16), sbuf(paa, "x_t1", [128, KC * 512], BF16)]
    esc = sbuf(paa, "esc", [128, 1024], F32)
    efx = sbuf(paa, "efx", [128, 1024], F32)
    mxs = sbuf(paa, "mxs", [128, 16], F32)
    sms = sbuf(paa, "sms", [128, 16], F32)
    rcs = sbuf(paa, "rcs", [128, 16], F32)
    v_ext = [sbuf(paa, "v_ext0", [128, 8 * 132], BF16), sbuf(paa, "v_ext1", [128, 8 * 132], BF16)]
    agg_sb = sbuf(paa, "agg_sb", [128, 8 * 132], F32)
    scdbg = sbuf(paa, "scdbg", [128, 1024], F32) if debug_taps else None
    aggdbg = sbuf(paa, "aggdbg", [128, 3 * 512], F32) if debug_taps else None
    vxall = sbuf(paa, "vxall", [128, 4 * NBLK], BF16) if debug_taps else None
    if debug_taps:
        op("vector", lambda e: e.memset(aggdbg[:], 0.0))
        op("vector", lambda e: e.memset(scdbg[:], 0.0))
        op("vector", lambda e: e.memset(vxall[:], 0.0))
    op("vector", lambda e: e.memset(agg_sb[:], 0.0))
    op("vector", lambda e: e.memset(v_ext[0][:], 1.0))
    op("vector", lambda e: e.memset(v_ext[1][:], 1.0))
    op("vector", lambda e: e.memset(avs[:], 0.0))

    def agg_slot(hp):
        return agg_ps[hp // 3], (hp % 3) * 132

    v_pre_v_end = cV.n
    v_pre_pe_end = cPE.n
    W("sync", cV, v_pre_v_end)
    W("sync", cPE, v_pre_pe_end)
    W("sync", sAc, v_wq_all)

    x_cast = {}
    x_pe_done = {}
    attn_done = {}
    vext_done = {}
    psv_free = {}
    pssc_free = {}
    agg_done = {}
    agg_accum = {}

    for tile in range(NTA):
        tb = tile % 2
        for kc in range(KC):
            sb_ = kc % 2
            prev = (tile * KC + kc) - 2
            if prev >= 0:
                W("sync", cV, x_cast[(prev // KC, prev % KC)])
            if kc == 0 and tile >= 2:
                W("sync", cPE, x_pe_done[tile - 2])
            vd = dma("sync", sSy, xstg[sb_][:], x_fm.ap()[kc * 128:(kc + 1) * 128, tile * 512:(tile + 1) * 512])
            W("vector", sSy, vd)
            x_cast[(tile, kc)] = op("vector", lambda e, sb_=sb_, tb=tb, kc=kc: e.tensor_copy(
                x_t[tb][:, kc * 512:(kc + 1) * 512], xstg[sb_][:]), inc=(cV, 1))

        for bo in range(4):
            blk = tile * 4 + bo
            if blk >= max_blks:
                continue
            W("tensor", cV, x_cast[(tile, KC - 1)])
            if blk == 0:
                W("tensor", cV, v_wembv_all)
                W("tensor", cV, v_wsc_all)
            if blk >= 1:
                W("tensor", cV, psv_free[blk - 1])
                W("tensor", cV, pssc_free[blk - 1])
                W("tensor", cV, agg_accum[blk - 1])
            v_emb = None
            for kc in range(KC):
                lhs = x_t[tb][:, kc * 512 + bo * 128: kc * 512 + (bo + 1) * 128]
                st, sp = (kc == 0), (kc == KC - 1)
                op("tensor", lambda e, lhs=lhs, kc=kc, st=st, sp=sp: e.matmul(
                    ps_sc[:, 0:512], lhs, wsc[:, kc * 1024: kc * 1024 + 512], start=st, stop=sp))
                op("tensor", lambda e, lhs=lhs, kc=kc, st=st, sp=sp: e.matmul(
                    ps_sc[:, 512:1024], lhs, wsc[:, kc * 1024 + 512:(kc + 1) * 1024], start=st, stop=sp))
                op("tensor", lambda e, lhs=lhs, kc=kc, st=st, sp=sp: e.matmul(
                    ps_v[:, 0:512], lhs, wembv[:, kc * 1024: kc * 1024 + 512], start=st, stop=sp))
                fn = lambda e, lhs=lhs, kc=kc, st=st, sp=sp: e.matmul(
                    ps_v[:, 512:1024], lhs, wembv[:, kc * 1024 + 512: (kc + 1) * 1024], start=st, stop=sp)
                if sp:
                    v_emb = op("tensor", fn, inc=(cPE, 1))
                else:
                    op("tensor", fn)
            if bo == 3:
                x_pe_done[tile] = v_emb

            W("vector", cPE, v_emb)
            if debug_taps and blk == NBLK - 1:
                W("scalar", cPE, v_emb)
                vtap = op("scalar", lambda e: e.copy(scdbg[:], ps_sc[:]), inc=(cS, 1))
                W("gpsimd", cS, vtap)
                dma("gpsimd", sPo, dbg["scores"].ap(), scdbg[:])
                W("vector", cS, vtap)
                vtap3 = op("scalar", lambda e: e.copy(aggdbg[:, 0:1024], ps_v[:]), inc=(cS, 1))
                W("gpsimd", cS, vtap3)
                dma("gpsimd", sPo, dbg["psv"].ap(), aggdbg[:, 0:1024])
            if no_sm:
                vsub = op("vector", lambda e: e.tensor_copy(esc[:], ps_sc[:]), inc=(cV, 1))
                pssc_free[blk] = vsub
                attn_done[blk] = op("vector", lambda e, blk=blk: e.memset(
                    attn[:, blk * 1024:(blk + 1) * 1024], 1.0), inc=(cV, 1))
            sc3 = ps_sc[:].rearrange("p (h a) -> p h a", h=16)
            e3s = esc[:].rearrange("p (h a) -> p h a", h=16)
            if no_sm:
                sc3 = None
            if not no_sm:
              vrm = op("vector", lambda e, sc3=sc3: e.reduce_max(mxs[:].unsqueeze(2), sc3, axis=AX), inc=(cV, 1))
              W("vector", cV, vrm)
            if not no_sm:
              mxb = mxs[:].unsqueeze(2).broadcast_to([128, 16, 64])
              vsub = op("vector", lambda e, sc3=sc3, mxb=mxb, e3s=e3s: e.tensor_tensor(
                  e3s, sc3, mxb, ALU.subtract), inc=(cV, 1))
              pssc_free[blk] = vsub
              W("scalar", cV, vsub)
              vexp = op("scalar", lambda e: e.activation(efx[:], esc[:], AF.Exp), inc=(cS, 1))
              W("vector", cS, vexp)
              f3 = efx[:].rearrange("p (h a) -> p h a", h=16)
              vrs_ = op("vector", lambda e, f3=f3: e.reduce_sum(sms[:].unsqueeze(2), f3, axis=AX), inc=(cV, 1))
              W("vector", cV, vrs_)
              vrc = op("vector", lambda e: e.reciprocal(rcs[:], sms[:]), inc=(cV, 1))
              W("vector", cV, vrc)
              rcb = rcs[:].unsqueeze(2).broadcast_to([128, 16, 64])
              attn_done[blk] = op("vector", lambda e, f3=f3, rcb=rcb, blk=blk: e.tensor_tensor(
                  attn[:, blk * 1024:(blk + 1) * 1024].rearrange("p (h a) -> p h a", h=16),
                  f3, rcb, ALU.mult), inc=(cV, 1))

            if blk >= 2:
                W("vector", cPE, agg_done[blk - 2])
            for hp in ([] if no_vext else range(8)):
                fn = lambda e, hp=hp, b=blk % 2: e.tensor_copy(
                    v_ext[b][:, hp * 132: hp * 132 + 128], ps_v[:, hp * 128:(hp + 1) * 128])
                if hp == 7:
                    vext_done[blk] = op("vector", fn, inc=(cV, 1))
                else:
                    op("vector", fn)
            if no_vext:
                vext_done[blk] = op("vector", lambda e: e.tensor_copy(recs[:, 0:1], recs[:, 0:1]), inc=(cV, 1))
            psv_free[blk] = vext_done[blk]
            if debug_taps and blk == NBLK - 1:
                W("gpsimd", cV, vext_done[blk])
                dma("gpsimd", sPo, dbg["vext"].ap(), v_ext[blk % 2][:])

            W("tensor", cV, attn_done[blk])
            W("tensor", cV, vext_done[blk])
            if debug_taps:
                W("gpsimd", cV, vext_done[blk])
                op("gpsimd", lambda e, blk=blk: e.tensor_copy(
                    vxall[:, blk * 4:(blk + 1) * 4], v_ext[blk % 2][:, 0:4]), inc=(cG, 1))
            if blk >= 1:
                W("tensor", cV, agg_accum[blk - 1])
            for hp in ([] if no_agg else range(8)):
                bank, off = agg_slot(hp)
                st = hp in (0, 3, 6)
                sp = hp in (2, 5, 7)
                fn = lambda e, hp=hp, bank=bank, off=off, st=st, sp=sp, blk=blk: e.matmul(
                    bank[:, off:off + 132], attn[:, blk * 1024 + hp * 128: blk * 1024 + (hp + 1) * 128],
                    v_ext[blk % 2][:, hp * 132: hp * 132 + 132], start=st, stop=sp)
                if hp == 7:
                    agg_done[blk] = op("tensor", fn, inc=(cPE, 1))
                else:
                    op("tensor", fn)
            if no_agg:
                agg_done[blk] = v_emb
            W("vector", cPE, agg_done[blk])
            op("vector", lambda e: e.tensor_tensor(agg_sb[:, 0:396], agg_sb[:, 0:396], agg_ps[0][:, 0:396], ALU.add))
            op("vector", lambda e: e.tensor_tensor(agg_sb[:, 396:792], agg_sb[:, 396:792], agg_ps[1][:, 0:396], ALU.add))
            agg_accum[blk] = op("vector", lambda e: e.tensor_tensor(
                agg_sb[:, 792:1056], agg_sb[:, 792:1056], agg_ps[2][:, 0:264], ALU.add), inc=(cV, 1))

    v_agg_all = agg_done[min(NBLK, max_blks) - 1]

    v_agg_accum_all = agg_accum[min(NBLK, max_blks) - 1]
    if debug_taps:
        W("gpsimd", cV, v_agg_accum_all)
        dma("gpsimd", sPo, dbg["aggps"].ap(), agg_sb[:])
    W("vector", cV, v_agg_accum_all)
    for hp in range(8):
        soff = hp * 132
        fn = lambda e, soff=soff, hp=hp: e.tensor_single_scalar(
            recs0[:, hp:hp + 1], agg_sb[:, soff + 128:soff + 129], 1e-35, ALU.max)
        if hp == 7:
            vr0 = op("vector", fn, inc=(cV, 1))
        else:
            op("vector", fn)
    W("vector", cV, vr0)
    vrg = op("vector", lambda e: e.reciprocal(recs[:], recs0[:]), inc=(cV, 1))
    W("vector", cV, vrg)
    v_avs = None
    for hp in range(8):
        soff = hp * 132
        op("vector", lambda e, soff=soff, hp=hp: e.tensor_single_scalar(
            avs[0:64, hp * 128: hp * 128 + 64], agg_sb[0:64, soff:soff + 64], recs[0:64, hp:hp + 1], ALU.mult))
        fn = lambda e, soff=soff, hp=hp: e.tensor_single_scalar(
            avs[64:128, hp * 128 + 64: hp * 128 + 128], agg_sb[64:128, soff + 64:soff + 128],
            recs[64:128, hp:hp + 1], ALU.mult)
        if hp == 7:
            v_avs = op("vector", fn, inc=(cV, 1))
        else:
            op("vector", fn)

    if debug_taps:
        W("gpsimd", cV, v_avs)
        dma("gpsimd", sPo, dbg["recs"].ap(), recs[:])
        dma("gpsimd", sPo, dbg["vxall"].ap(), vxall[:])
        W("gpsimd", cV, v_wsc_all)
        v_wsc_tap = dma("gpsimd", sPo, dbg["wsc"].ap(), wsc[:])
        W("gpsimd", cV, v_avs)
        dma("gpsimd", sPo, dbg["avs"].ap(), avs[:])
        W("gpsimd", cV, attn_done[min(NBLK, max_blks) - 1])
        v_dbgA = dma("gpsimd", sPo, dbg["attn"].ap(), attn[:])

    if max_stage <= 1:
        scopes.extend([paa, pa, pre, attn_scope])
        return _finish()

    paa.close()
    pa.close()
    pb = ExitStack()
    pt = psum_(pb, "pt", [128, 1024], BF16)
    mixed_ps = [psum_(pb, f"mx{i}", [128, 512]) for i in range(2)]
    wout_ps = psum_(pb, "wops", [128, 512])
    s1_ps = psum_(pb, "s1ps", [128, 512])
    s2_ps = psum_(pb, "s2ps", [128, 512])
    bcA_ps = psum_(pb, "bcA", [128, 512])
    bcB_ps = psum_(pb, "bcB", [128, 512])

    woutT_sb = sbuf(pb, "woutT_sb", [128, KC * 1024], BF16)
    attnT = sbuf(pb, "attnT", [128, 8 * 512], BF16)
    mixed_sb = sbuf(pb, "mixed_sb", [128, 8 * 512], BF16)
    r1 = sbuf(pb, "r1", [128, KC * 512], F32)
    y1t = sbuf(pb, "y1t", [128, KC * 512], BF16)
    r1b = sbuf(pb, "r1b", [128, 512], BF16)
    sqb = sbuf(pb, "sqb", [128, 512], BF16)
    xres = [sbuf(pb, "xres0", [128, 512], F32), sbuf(pb, "xres1", [128, 512], F32)]

    W("vector", cPE, v_agg_all)
    if debug_taps:
        W("vector", sPo, v_wsc_tap)
    wout_cast = []
    for kc in range(KC):
        b = kc % 2
        if kc >= 2:
            W("sync", cV, wout_cast[kc - 2])
        else:
            W("sync", cV, v_avs)
        vd = dma("sync", sSy, wstg[b][:, 0:1024], woutT.ap()[kc * 128:(kc + 1) * 128, :])
        W("vector", sSy, vd)
        wout_cast.append(op("vector", lambda e, b=b, kc=kc: e.tensor_copy(
            woutT_sb[:, kc * 1024:(kc + 1) * 1024], wstg[b][:, 0:1024]), inc=(cV, 1)))

    ln1_apply_done = {}
    ln1_stats_read = 0
    y1_spill = {}
    wout_evac_prev = 0
    r1_add = {}
    for tile in range(NTA):
        W("tensor", cPE, cPE.n)
        W("tensor", cV, v_avs)
        if tile > 0:
            W("tensor", cV, ln1_apply_done[tile - 1])
        if tile == 0:
            W("tensor", cV, wout_cast[-1])
        tr_evac = {}
        for bo in range(4):
            blk = tile * 4 + bo
            for hp in range(8):
                fn = lambda e, hp=hp, blk=blk: e.transpose(
                    pt[:, hp * 128:(hp + 1) * 128],
                    attn[:, blk * 1024 + hp * 128: blk * 1024 + (hp + 1) * 128], idt[:])
                if hp == 7:
                    v_tr = op("tensor", fn, inc=(cPE, 1))
                else:
                    op("tensor", fn)
            W("vector", cPE, v_tr)
            for hp in range(8):
                fn = lambda e, hp=hp, bo=bo: e.tensor_copy(
                    attnT[:, hp * 512 + bo * 128: hp * 512 + (bo + 1) * 128], pt[:, hp * 128:(hp + 1) * 128])
                if hp == 7:
                    tr_evac[bo] = op("vector", fn, inc=(cV, 1))
                else:
                    op("vector", fn)
            W("tensor", cV, tr_evac[bo])
        v_trans = tr_evac[3]

        mixed_evac = {}
        for hp in range(8):
            if hp >= 1:
                W("tensor", cV, mixed_evac[hp - 1])
            vmm = op("tensor", lambda e, hp=hp: e.matmul(
                mixed_ps[hp % 2][:], avs[:, hp * 128:(hp + 1) * 128], attnT[:, hp * 512:(hp + 1) * 512],
                start=True, stop=True), inc=(cPE, 1))
            W("vector", cPE, vmm)
            mixed_evac[hp] = op("vector", lambda e, hp=hp: e.tensor_copy(
                mixed_sb[:, hp * 512:(hp + 1) * 512], mixed_ps[hp % 2][:]), inc=(cV, 1))
        if debug_taps and tile == 0:
            W("gpsimd", cV, mixed_evac[7])
            dma("gpsimd", sPo, dbg["mixed"].ap(), mixed_sb[:])

        xres_dma = {}
        for mo in range(KC):
            b = mo % 2
            prev = mo - 2
            if prev >= 0:
                W("sync", cV, r1_add[(tile, prev)])
            elif tile > 0:
                W("sync", cV, r1_add[(tile - 1, KC + prev)])
            else:
                W("sync", cV, v_avs)
            xres_dma[mo] = dma("sync", sSy, xres[b][:],
                               x_fm.ap()[mo * 128:(mo + 1) * 128, tile * 512:(tile + 1) * 512])
            W("tensor", cV, mixed_evac[7])
            if mo >= 1 or tile > 0:
                W("tensor", cV, wout_evac_prev)
            for kc in range(KC):
                st, sp = (kc == 0), (kc == KC - 1)
                fn = lambda e, mo=mo, kc=kc, st=st, sp=sp: e.matmul(
                    wout_ps[:], woutT_sb[:, kc * 1024 + mo * 128: kc * 1024 + (mo + 1) * 128],
                    mixed_sb[:, kc * 512:(kc + 1) * 512], start=st, stop=sp)
                if sp:
                    vmm = op("tensor", fn, inc=(cPE, 1))
                else:
                    op("tensor", fn)
            W("vector", cPE, vmm)
            if tile > 0:
                W("vector", cV, ln1_apply_done[tile - 1])
            W("vector", sSy, xres_dma[mo])
            op("vector", lambda e, mo=mo, b=mo % 2: e.tensor_tensor(
                r1[:, mo * 512:(mo + 1) * 512], wout_ps[:], xres[b][:], ALU.add))
            r1_add[(tile, mo)] = op("vector", lambda e, mo=mo: e.tensor_single_scalar(
                r1[:, mo * 512:(mo + 1) * 512], r1[:, mo * 512:(mo + 1) * 512],
                bout_sb[:, mo:mo + 1], ALU.add), inc=(cV, 1))
            wout_evac_prev = r1_add[(tile, mo)]

        for mo in range(KC):
            W("vector", cV, r1_add[(tile, mo)])
            vb = op("vector", lambda e, mo=mo: e.tensor_copy(r1b[:], r1[:, mo * 512:(mo + 1) * 512]), inc=(cV, 1))
            W("scalar", cV, r1_add[(tile, mo)])
            vs_ = op("scalar", lambda e, mo=mo: e.activation(
                sqb[:], r1[:, mo * 512:(mo + 1) * 512], AF.Square), inc=(cS, 1))
            W("tensor", cV, vb)
            W("tensor", cS, vs_)
            if mo == 0 and ln1_stats_read:
                W("tensor", cV, ln1_stats_read)
            st, sp = (mo == 0), (mo == KC - 1)
            op("tensor", lambda e, st=st, sp=sp: e.matmul(s1_ps[0:1, :], onesb[:, 0:1], r1b[:], start=st, stop=sp))
            vmm2 = op("tensor", lambda e, st=st, sp=sp: e.matmul(
                s2_ps[0:1, :], onesb[:, 0:1], sqb[:], start=st, stop=sp), inc=(cPE, 1))
            W("vector", cPE, vmm2)
            W("scalar", cPE, vmm2)

        vmean = op("vector", lambda e: e.tensor_single_scalar(
            meanr[0:1, :], s1_ps[0:1, :], 1.0 / 1024.0, ALU.mult), inc=(cV, 1))
        W("scalar", cV, vmean)
        vmsq = op("scalar", lambda e: e.activation(msqr[0:1, :], meanr[0:1, :], AF.Square), inc=(cS, 1))
        W("vector", cS, vmsq)
        op("vector", lambda e: e.scalar_tensor_tensor(
            varr[0:1, :], s2_ps[0:1, :], 1.0 / 1024.0, msqr[0:1, :], ALU.mult, ALU.subtract))
        ln1_stats_read = op("vector", lambda e: e.tensor_single_scalar(
            varr[0:1, :], varr[0:1, :], 1e-5, ALU.add), inc=(cV, 1))
        W("scalar", cV, ln1_stats_read)
        vrs = op("scalar", lambda e: e.activation(varr[0:1, :], varr[0:1, :], AF.Sqrt), inc=(cS, 1))
        W("vector", cS, vrs)
        vrr = op("vector", lambda e: e.reciprocal(rsigr[0:1, :], varr[0:1, :]), inc=(cV, 1))
        W("vector", cV, vrr)
        op("vector", lambda e: e.tensor_copy(abrow[0:1, 0:512], rsigr[0:1, :]))
        vab = op("vector", lambda e: e.scalar_tensor_tensor(
            abrow[0:1, 512:1024], meanr[0:1, :], -1.0, rsigr[0:1, :], ALU.mult, ALU.mult), inc=(cV, 1))

        W("tensor", cV, vab)
        if tile > 0:
            W("tensor", cV, ln1_apply_done[tile - 1])
        op("tensor", lambda e: e.matmul(bcA_ps[:], onesf[0:1, :], abrow[0:1, 0:512], start=True, stop=True))
        vbc = op("tensor", lambda e: e.matmul(bcB_ps[:], onesf[0:1, :], abrow[0:1, 512:1024],
                                              start=True, stop=True), inc=(cPE, 1))
        for mo in range(KC):
            W("vector", cPE, vbc)
            op("vector", lambda e, mo=mo: e.tensor_tensor(
                r1[:, mo * 512:(mo + 1) * 512], r1[:, mo * 512:(mo + 1) * 512], bcA_ps[:], ALU.mult))
            if tile > 0:
                W("vector", sPo, y1_spill[mo])
            vad = op("vector", lambda e, mo=mo: e.tensor_tensor(
                y1t[:, mo * 512:(mo + 1) * 512],
                r1[:, mo * 512:(mo + 1) * 512], bcB_ps[:], ALU.add), inc=(cV, 1))
            W("gpsimd", cV, vad)
            y1_spill[mo] = dma("gpsimd", sPo, y1_dram.ap()[mo * 128:(mo + 1) * 128, tile * 512:(tile + 1) * 512],
                               y1t[:, mo * 512:(mo + 1) * 512])
        ln1_apply_done[tile] = vad
    v_y1_all = ln1_apply_done[NTA - 1]
    if out_stage == "y1":
        W("vector", cV, v_y1_all)
        zscr = sbuf(pb, "zscr_y1", [128, 512], F32)
        zscrb = sbuf(pb, "zscrb_y1", [128, 512], BF16)
        vdmas = 0
        for kc in range(KC):
            for tt in range(NTA):
                vin = dma("sync", sSy, zscrb[:], y1_dram.ap()[kc * 128:(kc + 1) * 128, tt * 512:(tt + 1) * 512])
                W("vector", sSy, vin)
                if vdmas:
                    W("vector", sSy, vdmas)
                vv = op("vector", lambda e: e.tensor_copy(zscr[:], zscrb[:]), inc=(cV, 1))
                W("sync", cV, vv)
                vdmas = dma("sync", sSy, z_fm.ap()[kc * 128:(kc + 1) * 128, tt * 512:(tt + 1) * 512], zscr[:])
        scopes.extend([pb, pre, attn_scope])
        return _finish()


    if max_stage <= 2:
        scopes.extend([pb, pre, attn_scope])
        return _finish()

    pb.close()
    attn_scope.close()
    pc = ExitStack()
    f1_ps = [psum_(pc, f"f1ps{i}", [128, 512]) for i in range(2)]
    f2_ps = [psum_(pc, f"f2ps{i}", [128, 512]) for i in range(2)]
    s1b_ps = psum_(pc, "s1b", [128, 512])
    s2b_ps = psum_(pc, "s2b", [128, 512])
    bcA2_ps = psum_(pc, "bcA2", [128, 512])
    bcB2_ps = psum_(pc, "bcB2", [128, 512])

    t_sb = sbuf(pc, "t_sb", [128, KC * 512], BF16)
    y1e = sbuf(pc, "y1e", [128, KC * 512], BF16)
    h_sb = sbuf(pc, "h_sb", [128, MC1 * 512], BF16)
    r2 = sbuf(pc, "r2", [128, KC * 512], BF16)
    w1m_sb = [sbuf(pc, "w1m0", [128, 1024], BF16), sbuf(pc, "w1m1", [128, 1024], BF16)]
    w2m_sb = [sbuf(pc, "w2m0", [128, 4096], BF16), sbuf(pc, "w2m1", [128, 4096], BF16)]
    x16 = [sbuf(pc, "x16_0", [128, 512], F32), sbuf(pc, "x16_1", [128, 512], F32)]
    qAs = [sbuf(pc, "qA0", [128, 512], F32), sbuf(pc, "qA1", [128, 512], F32)]
    qBs = [sbuf(pc, "qB0", [128, 512], F32), sbuf(pc, "qB1", [128, 512], F32)]
    xh = [sbuf(pc, "xh0", [128, 512], F32), sbuf(pc, "xh1", [128, 512], F32)]
    o2s = [sbuf(pc, "o2s0", [128, 512], F32), sbuf(pc, "o2s1", [128, 512], F32)]
    z1s = [sbuf(pc, "z1s0", [128, 512], F32), sbuf(pc, "z1s1", [128, 512], F32)]
    zs = [sbuf(pc, "zs0", [128, 512], F32), sbuf(pc, "zs1", [128, 512], F32)]
    r2b2 = sbuf(pc, "r2b2", [128, 512], BF16)
    ln2dbg = sbuf(pc, "ln2dbg", [128, 1024], F32) if debug_taps else None
    ln2rowsdbg = sbuf(pc, "ln2rowsdbg", [1, 2048], F32) if debug_taps else None

    t_pe_done = {}
    h_pe_done = {}
    w1_pe = {}
    w2_pe = {}
    ln2_stats_read = 0
    ln2_apply_done = 0
    z_dma = {}
    z1_read = {}
    tq = {}
    x16_done = {}
    x16_done_prev = {}

    v_b1_pe_end = cPE.n
    y1e_dma = {}
    for et in range(NE):
        if et == 0:
            W("sync", cV, v_y1_all)
            W("sync", sPo, y1_spill[KC - 1])
        for kc in range(KC):
            if et > 0:
                W("sync", cV, r2_prev[kc])
            y1e_dma[kc] = dma("sync", sSy, y1e[:, kc * 512:(kc + 1) * 512],
                              y1_dram.ap()[kc * 128:(kc + 1) * 128, et * 512:(et + 1) * 512])
        if et == 0:
            W("scalar", cV, v_y1_all)
            W("scalar", cPE, v_b1_pe_end)
        for kc in range(KC):
            b = kc % 2
            gi = et * KC + kc
            if gi >= 2:
                W("scalar", cV, tq[gi - 2])
            W("scalar", sSy, y1e_dma[kc])
            vx = op("scalar", lambda e, kc=kc, b=b: e.activation(
                x16[b][:], y1e[:, kc * 512:(kc + 1) * 512], AF.Copy, scale=16.0), inc=(cS, 1))
            x16_done[kc] = vx
            W("vector", cS, vx)
            if kc == 0 and et > 0:
                W("vector", cPE, t_pe_done[et - 1])
            v_ = op("vector", lambda e, b=b: e.tensor_single_scalar(qAs[b][:], x16[b][:], M23, ALU.add), inc=(cV, 1))
            W("vector", cV, v_)
            v_ = op("vector", lambda e, b=b: e.tensor_scalar(qAs[b][:], qAs[b][:], -M23, 127.0, ALU.add, ALU.min), inc=(cV, 1))
            W("vector", cV, v_)
            v_ = op("vector", lambda e, b=b: e.tensor_single_scalar(qBs[b][:], x16[b][:], M52, ALU.add), inc=(cV, 1))
            W("vector", cV, v_)
            v_ = op("vector", lambda e, b=b: e.tensor_scalar(
                qBs[b][:], qBs[b][:], -M52, 127.0, ALU.add, ALU.min), inc=(cV, 1))
            W("vector", cV, v_)
            vcmb = op("vector", lambda e, b=b: e.scalar_tensor_tensor(
                qAs[b][:], qAs[b][:], fflag[:, 0:1], qBs[b][:], ALU.mult, ALU.add), inc=(cV, 1))
            W("vector", cV, vcmb)
            tq[gi] = op("vector", lambda e, b=b, kc=kc: e.tensor_scalar(
                t_sb[:, kc * 512:(kc + 1) * 512], qAs[b][:], -128.0, 127.0, ALU.max, ALU.min), inc=(cV, 1))
        v_tq = tq[et * KC + KC - 1]
        if out_stage == "t" and et == 0:
            W("vector", cV, v_tq)
            zscr_t = sbuf(pc, "zscr_t", [128, 512], F32)
            vdm = 0
            for kc in range(KC):
                if vdm:
                    W("vector", sSy, vdm)
                vv = op("vector", lambda e, kc=kc: e.tensor_copy(zscr_t[:], t_sb[:, kc * 512:(kc + 1) * 512]), inc=(cV, 1))
                W("sync", cV, vv)
                vdm = dma("sync", sSy, z_fm.ap()[kc * 128:(kc + 1) * 128, 0:512], zscr_t[:])
            scopes.extend([pc, pre])
            return _finish()
        if debug_taps and et == 0:
            W("gpsimd", cV, v_tq)
            dma("gpsimd", sPo, dbg["t"].ap(), t_sb[:])

        f1_evac = {}
        h_done = {}
        for m in range(MC1):
            wb = m % 2
            gm = et * MC1 + m
            if gm >= 2:
                W("scalar", cPE, w1_pe[gm - 2])
            if gm == 0:
                W("scalar", sAc, v_wq_all)
            vd = dma("scalar", sAc, w1m_sb[wb][:], w1q_dram.ap()[m])
            if debug_taps and et == 0 and m == 0:
                W("gpsimd", sAc, vd)
                dma("gpsimd", sPo, dbg["w1m"].ap(), w1m_sb[wb][:])
            W("tensor", sAc, vd)
            if m == 0:
                W("tensor", cV, v_tq)
                if et > 0:
                    W("tensor", cV, h_q_last_prev)
                    W("tensor", cV, ln2_apply_done)
            if m >= 1:
                W("tensor", cV, f1_evac[m - 1])
            for kc in range(KC):
                st, sp = (kc == 0), (kc == KC - 1)
                fn = lambda e, m=m, kc=kc, st=st, sp=sp: e.matmul(
                    f1_ps[m % 2][:], w1m_sb[m % 2][:, kc * 128:(kc + 1) * 128],
                    t_sb[:, kc * 512:(kc + 1) * 512], start=st, stop=sp)
                if sp:
                    vmm = op("tensor", fn, inc=(cPE, 1))
                else:
                    op("tensor", fn)
            w1_pe[gm] = vmm
            if out_stage == "f1ps" and et == 0 and m == 0:
                W("vector", cPE, vmm)
                zscr_f = sbuf(pc, "zscr_f", [128, 512], F32)
                vv = op("vector", lambda e: e.tensor_copy(zscr_f[:], f1_ps[0][:]), inc=(cV, 1))
                W("sync", cV, vv)
                dma("sync", sSy, z_fm.ap()[0:128, 0:512], zscr_f[:])
                scopes.extend([pc, pre])
                return _finish()
            W("vector", cPE, vmm)
            if m >= 2:
                W("vector", cV, h_done[m - 2])
            if m == 0 and et > 0:
                W("vector", cPE, h_pe_done[et - 1])
            f1_evac[m] = op("vector", lambda e, m=m: e.scalar_tensor_tensor(
                xh[m % 2][:], f1_ps[m % 2][:], 1.0 / 64.0,
                b1e[:, m:m + 1].broadcast_to([128, 512]), ALU.mult, ALU.add), inc=(cV, 1))
            if out_stage == "xh" and et == 0 and m == 0:
                W("vector", cV, f1_evac[m])
                zscr_x = sbuf(pc, "zscr_x", [128, 512], F32)
                vv = op("vector", lambda e: e.tensor_copy(zscr_x[:], xh[0][:]), inc=(cV, 1))
                W("sync", cV, vv)
                dma("sync", sSy, z_fm.ap()[0:128, 0:512], zscr_x[:])
                scopes.extend([pc, pre])
                return _finish()
            v_ = op("vector", lambda e, m=m: e.tensor_scalar(qAs[m % 2][:], xh[m % 2][:], 0.0, M23, ALU.max, ALU.add), inc=(cV, 1))
            W("vector", cV, v_)
            v_ = op("vector", lambda e, m=m: e.tensor_scalar(qAs[m % 2][:], qAs[m % 2][:], -M23, 127.0, ALU.add, ALU.min), inc=(cV, 1))
            W("vector", cV, v_)
            v_ = op("vector", lambda e, m=m: e.tensor_scalar(qBs[m % 2][:], xh[m % 2][:], 0.0, M52, ALU.max, ALU.add), inc=(cV, 1))
            W("vector", cV, v_)
            v_ = op("vector", lambda e, m=m: e.tensor_scalar(
                qBs[m % 2][:], qBs[m % 2][:], -M52, 127.0, ALU.add, ALU.min), inc=(cV, 1))
            W("vector", cV, v_)
            h_done[m] = op("vector", lambda e, m=m: e.scalar_tensor_tensor(
                h_sb[:, m * 512:(m + 1) * 512], qAs[m % 2][:], fflag[:, 0:1], qBs[m % 2][:],
                ALU.mult, ALU.add), inc=(cV, 1))
        t_pe_done[et] = w1_pe[et * MC1 + MC1 - 1]
        v_h_all = h_done[MC1 - 1]
        if out_stage == "h" and et == 0:
            W("vector", cV, v_h_all)
            zscr_h = sbuf(pc, "zscr_h", [128, 512], F32)
            vdm = 0
            for kc in range(KC):
                if vdm:
                    W("vector", sSy, vdm)
                vv = op("vector", lambda e, kc=kc: e.tensor_copy(zscr_h[:], h_sb[:, kc * 512:(kc + 1) * 512]), inc=(cV, 1))
                W("sync", cV, vv)
                vdm = dma("sync", sSy, z_fm.ap()[kc * 128:(kc + 1) * 128, 0:512], zscr_h[:])
            scopes.extend([pc, pre])
            return _finish()
        h_q_last_prev = v_h_all
        if debug_taps and et == 0:
            W("gpsimd", cV, v_h_all)
            dma("gpsimd", sPo, dbg["h"].ap(), h_sb[:])

        f2_evac = {}
        r2_done = {}
        for m2 in range(KC):
            wb = m2 % 2
            gm = et * KC + m2
            if gm >= 2:
                W("scalar", cPE, w2_pe[gm - 2])
            vd = dma("scalar", sAc, w2m_sb[wb][:], w2q_dram.ap()[m2])
            if debug_taps and et == 0 and m2 == 0:
                W("gpsimd", sAc, vd)
                dma("gpsimd", sPo, dbg["w2m"].ap(), w2m_sb[wb][:])
            W("tensor", sAc, vd)
            if m2 == 0:
                W("tensor", cV, v_h_all)
            if m2 >= 1:
                W("tensor", cV, f2_evac[m2 - 1])
            for kk in range(MC1):
                st, sp = (kk == 0), (kk == MC1 - 1)
                fn = lambda e, m2=m2, kk=kk, st=st, sp=sp: e.matmul(
                    f2_ps[m2 % 2][:], w2m_sb[m2 % 2][:, kk * 128:(kk + 1) * 128],
                    h_sb[:, kk * 512:(kk + 1) * 512], start=st, stop=sp)
                if sp:
                    vmm = op("tensor", fn, inc=(cPE, 1))
                else:
                    op("tensor", fn)
            w2_pe[gm] = vmm
            W("vector", cPE, vmm)
            f2_evac[m2] = op("vector", lambda e, m2=m2: e.scalar_tensor_tensor(
                o2s[m2 % 2][:], f2_ps[m2 % 2][:], 1.0 / 1024.0,
                b2e[:, m2:m2 + 1].broadcast_to([128, 512]), ALU.mult, ALU.add), inc=(cV, 1))
            r2_done[m2] = op("vector", lambda e, m2=m2: e.tensor_tensor(
                r2[:, m2 * 512:(m2 + 1) * 512], o2s[m2 % 2][:],
                y1e[:, m2 * 512:(m2 + 1) * 512], ALU.add), inc=(cV, 1))
        h_pe_done[et] = w2_pe[et * KC + KC - 1]
        r2_prev = dict(r2_done)
        if out_stage == "r2" and et == 0:
            W("vector", cV, r2_done[KC - 1])
            zscr_r = sbuf(pc, "zscr_r", [128, 512], F32)
            vdm = 0
            for kc in range(KC):
                if vdm:
                    W("vector", sSy, vdm)
                vv = op("vector", lambda e, kc=kc: e.tensor_copy(zscr_r[:], r2[:, kc * 512:(kc + 1) * 512]), inc=(cV, 1))
                W("sync", cV, vv)
                vdm = dma("sync", sSy, z_fm.ap()[kc * 128:(kc + 1) * 128, 0:512], zscr_r[:])
            scopes.extend([pc, pre])
            return _finish()

        for mo in range(KC):
            W("scalar", cV, r2_done[mo])
            if mo > 0 or et > 0:
                pass
            vs_ = op("scalar", lambda e, mo=mo: e.activation(
                r2b2[:], r2[:, mo * 512:(mo + 1) * 512], AF.Square), inc=(cS, 1))
            W("tensor", cV, r2_done[mo])
            W("tensor", cS, vs_)
            if mo == 0 and ln2_stats_read:
                W("tensor", cV, ln2_stats_read)
            st, sp = (mo == 0), (mo == KC - 1)
            op("tensor", lambda e, mo=mo, st=st, sp=sp: e.matmul(
                s1b_ps[0:1, :], onesb[:, 0:1], r2[:, mo * 512:(mo + 1) * 512], start=st, stop=sp))
            vmm2 = op("tensor", lambda e, st=st, sp=sp: e.matmul(
                s2b_ps[0:1, :], onesb[:, 0:1], r2b2[:], start=st, stop=sp), inc=(cPE, 1))
            W("scalar", cPE, vmm2)
        W("vector", cPE, vmm2)
        vmean = op("vector", lambda e: e.tensor_single_scalar(
            meanr[0:1, :], s1b_ps[0:1, :], 1.0 / 1024.0, ALU.mult), inc=(cV, 1))
        W("scalar", cV, vmean)
        vmsq = op("scalar", lambda e: e.activation(msqr[0:1, :], meanr[0:1, :], AF.Square), inc=(cS, 1))
        W("vector", cS, vmsq)
        op("vector", lambda e: e.scalar_tensor_tensor(
            varr[0:1, :], s2b_ps[0:1, :], 1.0 / 1024.0, msqr[0:1, :], ALU.mult, ALU.subtract))
        ln2_stats_read = op("vector", lambda e: e.tensor_single_scalar(
            varr[0:1, :], varr[0:1, :], 1e-5, ALU.add), inc=(cV, 1))
        W("scalar", cV, ln2_stats_read)
        vrs = op("scalar", lambda e: e.activation(varr[0:1, :], varr[0:1, :], AF.Sqrt), inc=(cS, 1))
        W("vector", cS, vrs)
        vrr = op("vector", lambda e: e.reciprocal(rsigr[0:1, :], varr[0:1, :]), inc=(cV, 1))
        W("vector", cV, vrr)
        op("vector", lambda e: e.tensor_copy(abrow[0:1, 0:512], rsigr[0:1, :]))
        vab = op("vector", lambda e: e.scalar_tensor_tensor(
            abrow[0:1, 512:1024], meanr[0:1, :], -1.0, rsigr[0:1, :], ALU.mult, ALU.mult), inc=(cV, 1))
        W("tensor", cV, vab)
        if ln2_apply_done:
            W("tensor", cV, ln2_apply_done)
        op("tensor", lambda e: e.matmul(bcA2_ps[:], onesf[0:1, :], abrow[0:1, 0:512], start=True, stop=True))
        vbc = op("tensor", lambda e: e.matmul(bcB2_ps[:], onesf[0:1, :], abrow[0:1, 512:1024],
                                              start=True, stop=True), inc=(cPE, 1))
        if debug_taps and et == 0:
            W("vector", cPE, vbc)
            op("vector", lambda e: e.tensor_copy(ln2dbg[:, 0:512], bcA2_ps[:]))
            vt2 = op("vector", lambda e: e.tensor_copy(ln2dbg[:, 512:1024], bcB2_ps[:]), inc=(cV, 1))
            op("vector", lambda e: e.tensor_copy(ln2rowsdbg[0:1, 0:512], meanr[0:1, :]))
            op("vector", lambda e: e.tensor_copy(ln2rowsdbg[0:1, 512:1024], varr[0:1, :]))
            op("vector", lambda e: e.tensor_copy(ln2rowsdbg[0:1, 1024:1536], rsigr[0:1, :]))
            vt3 = op("vector", lambda e: e.tensor_copy(ln2rowsdbg[0:1, 1536:2048], abrow[0:1, 0:512]), inc=(cV, 1))
            W("gpsimd", cV, vt3)
            dma("gpsimd", sPo, dbg["ln2"].ap(), ln2dbg[:])
            dma("gpsimd", sPo, dbg["ln2rows"].ap(), ln2rowsdbg[:])
        for mo in range(KC):
            b = mo % 2
            gi = et * KC + mo
            W("vector", cPE, vbc)
            op("vector", lambda e, mo=mo, b=b: e.tensor_tensor(
                z1s[b][:], r2[:, mo * 512:(mo + 1) * 512], bcA2_ps[:], ALU.mult))
            if gi >= 2:
                W("vector", sPo, z_dma[gi - 2])
            z1_read[gi] = op("vector", lambda e, mo=mo, b=b: e.tensor_tensor(
                zs[b][:], z1s[b][:], bcB2_ps[:], ALU.add), inc=(cV, 1))
            W("gpsimd", cV, z1_read[gi])
            z_dma[gi] = dma("gpsimd", sPo, z_fm.ap()[mo * 128:(mo + 1) * 128, et * 512:(et + 1) * 512], zs[b][:])
            if debug_taps and gi == 0:
                W("gpsimd", cV, z1_read[gi])
                dma("gpsimd", sPo, dbg["zstap"].ap(), zs[b][:])
        if debug_taps and et == 0:
            W("gpsimd", cV, r2_done[KC - 1])
            dma("gpsimd", sPo, dbg["r2tap"].ap(), r2[:])
        ln2_apply_done = z1_read[et * KC + KC - 1]

    scopes.extend([pc, pre])
    return _finish()


def _bf16():
    import ml_dtypes
    return ml_dtypes.bfloat16


def _prep_inputs(L0_attn_x, L0_attn_W_emb, L0_attn_W_anchor, L0_attn_W_out,
                 L0_attn_b_out, L0_ffn_w1, L0_ffn_b1, L0_ffn_w2, L0_ffn_b2):
    c = np.ascontiguousarray
    base = {
        "wemb_qk": c(np.asarray(L0_attn_W_emb)[0:D]),
        "wembT_v": c(np.asarray(L0_attn_W_emb)[D:2 * D].T),
        "wanchT": c(np.asarray(L0_attn_W_anchor).T),
        "woutT": c(np.asarray(L0_attn_W_out).T),
        "bout_pc": c(np.asarray(L0_attn_b_out).reshape(KC, 128).T),
        "w1T": c(np.asarray(L0_ffn_w1).T),
        "b1_pc": c(np.asarray(L0_ffn_b1).reshape(MC1, 128).T),
        "w2T": c(np.asarray(L0_ffn_w2).T),
        "b2_pc": c(np.asarray(L0_ffn_b2).reshape(KC, 128).T),
        "iden": np.eye(128, dtype=np.float32).astype(_bf16()),
    }
    x = np.asarray(L0_attn_x)
    return [dict(base, x_fm=c(x[i].T)) for i in range(x.shape[0])]


def kernel(L0_attn_x, L0_attn_W_emb, L0_attn_W_anchor, L0_attn_W_out,
           L0_attn_b_out, L0_ffn_w1, L0_ffn_b1, L0_ffn_w2, L0_ffn_b2):
    in_maps = _prep_inputs(L0_attn_x, L0_attn_W_emb, L0_attn_W_anchor,
                           L0_attn_W_out, L0_attn_b_out, L0_ffn_w1, L0_ffn_b1,
                           L0_ffn_w2, L0_ffn_b2)
    nc = build_nc(n_tokens=4096)
    res = run_bass_kernel_spmd(nc, in_maps, core_ids=list(range(8)))
    out = np.empty((8, 4096, 1024), dtype=np.float32)
    for i in range(8):
        out[i] = res.results[i]["z_fm"].T
    return out
